# version 19
# baseline (speedup 1.0000x reference)
import math
import numpy as np
import concourse.bass as bass
import concourse.mybir as mybir
from concourse.bass_utils import run_bass_kernel_spmd

F32 = mybir.dt.float32
BF16 = mybir.dt.bfloat16
AF = mybir.ActivationFunctionType
OP = mybir.AluOpType

D = 1024
NT = 18
NMAIN = 16
NMACRO = 8
STD = [1.0, 2.0, 4.0, 8.0]
OFF = [-1.0, -2.0, -4.0, -8.0]
EPS = 1e-5


def build_nc(n_macros=NMACRO, debug_dump=False, scratch=8192):
    nc = bass.Bass("TRN2", target_bir_lowering=False,
                   dynamic_dma_scratch_size=scratch,
                   detect_race_conditions=False)

    xs = nc.dram_tensor("xs", [NT * 128, D], F32, kind="ExternalInput").ap()
    wint = nc.dram_tensor("wint", [128, 8, D], F32, kind="ExternalInput").ap()
    woutt = nc.dram_tensor("woutt", [128, 8, D], F32, kind="ExternalInput").ap()
    w1 = nc.dram_tensor("w1", [128, 32, 8, 128], F32, kind="ExternalInput").ap()
    w2 = nc.dram_tensor("w2", [128, 32, D], F32, kind="ExternalInput").ap()
    b1t = nc.dram_tensor("b1t", [128, 32], F32, kind="ExternalInput").ap()
    b2bc = nc.dram_tensor("b2bc", [128, D], F32, kind="ExternalInput").ap()
    outh = nc.dram_tensor("out", [NMAIN * 128, D], F32, kind="ExternalOutput").ap()

    W1s = nc.alloc_sbuf_tensor("W1s", [128, 32, 8, 128], BF16)
    W2s = nc.alloc_sbuf_tensor("W2s", [128, 32, 1024], BF16)
    WinS = nc.alloc_sbuf_tensor("WinS", [128, 8, 1024], BF16)
    WoutS = nc.alloc_sbuf_tensor("WoutS", [128, 8, 1024], BF16)
    xln = nc.alloc_sbuf_tensor("xln", [128, 3, D], F32)
    outsb = nc.alloc_sbuf_tensor("outsb", [128, 2, D], F32)
    xnb = nc.alloc_sbuf_tensor("xnb", [128, 2, D], BF16)
    xnt = nc.alloc_sbuf_tensor("xnt", [128, 2, 8, 256], BF16)
    vring = nc.alloc_sbuf_tensor("vring", [128, 4, D], BF16)
    attb = nc.alloc_sbuf_tensor("attb", [128, 2, D], BF16)
    h1r = nc.alloc_sbuf_tensor("h1r", [128, 8, 256], BF16)
    Pt = nc.alloc_sbuf_tensor("Pt", [128, 12, 128], BF16)
    ident = nc.alloc_sbuf_tensor("ident", [128, 128], BF16)
    b1s = nc.alloc_sbuf_tensor("b1s", [128, 32], F32)
    b2s = nc.alloc_sbuf_tensor("b2s", [128, D], F32)
    mvs = nc.alloc_sbuf_tensor("mvs", [128, 2, 10], F32)
    stat = nc.alloc_sbuf_tensor("stat", [128, 2, 12], F32)
    lninv = nc.alloc_sbuf_tensor("lninv", [128, 4], F32)

    feat = nc.alloc_psum_tensor("feat", [128, 1024], F32)
    outA = nc.alloc_psum_tensor("outA", [128, 1024], F32)
    outB = nc.alloc_psum_tensor("outB", [128, 1024], F32)
    f1p = [nc.alloc_psum_tensor("f1a", [128, 512], F32),
           nc.alloc_psum_tensor("f1b", [128, 512], F32)]

    sem_names = ["pe", "vs", "sc", "gp", "xin", "pre", "sto",
                 "w_win", "w_w1a", "w_w1b", "w_wout", "w_w2a", "w_w2b",
                 "w_b1", "w_b2"]
    S = {n: nc.alloc_semaphore("sem_" + n) for n in sem_names}

    cnt = {n: 0 for n in sem_names}
    after = {}
    ops = {k: [] for k in ("tensor", "vector", "scalar", "sync", "gpsimd")}

    def emit(eng, fn):
        ops[eng].append(fn)

    def bump(sem, event=None, n=1):
        cnt[sem] += n
        if event is not None:
            assert (sem, event) not in after, event
            after[(sem, event)] = cnt[sem]
        return cnt[sem]

    def wt(sem, event):
        v = after.get((sem, event))
        return None if v is None else (sem, v)

    def wt_req(sem, event):
        v = after[(sem, event)]
        return (sem, v)

    def mk_waits(e, waits):
        for w in waits:
            if w is None:
                continue
            sn, val = w
            e.wait_ge(S[sn], val)

    def t_mms(waits, mms, event=None):
        do_inc = event is not None
        if do_inc:
            bump("pe", event)

        def fn(e, waits=tuple(waits), mms=tuple(mms), do_inc=do_inc):
            mk_waits(e, waits)
            last = None
            for (o, l, r, st, sp) in mms:
                last = e.matmul(o, l, r, start=st, stop=sp,
                                skip_group_check=True)
            if do_inc:
                last.then_inc(S["pe"], 1)
        emit("tensor", fn)

    def v_op(waits, opfn, event=None):
        do_inc = event is not None
        if do_inc:
            bump("vs", event)

        def fn(e, waits=tuple(waits), opfn=opfn, do_inc=do_inc):
            mk_waits(e, waits)
            ins = opfn(e)
            if do_inc:
                if ins is None:
                    ins = e.drain()
                ins.then_inc(S["vs"], 1)
        emit("vector", fn)

    def s_op(waits, opfn, event=None):
        do_inc = event is not None
        if do_inc:
            bump("sc", event)

        def fn(e, waits=tuple(waits), opfn=opfn, do_inc=do_inc):
            mk_waits(e, waits)
            ins = opfn(e)
            if do_inc:
                if ins is None:
                    ins = e.drain()
                ins.then_inc(S["sc"], 1)
        emit("scalar", fn)

    def dma(eng, waits, out_ap, in_ap, sem, event=None, accum=False):
        bump(sem, event, 16)

        def fn(e, waits=tuple(waits), o=out_ap, i=in_ap, sem=sem, accum=accum):
            mk_waits(e, waits)
            if accum:
                e.dma_start(out=o, in_=i, accum_op=OP.add).then_inc(S[sem], 16)
            else:
                e.dma_start(out=o, in_=i).then_inc(S[sem], 16)
        emit(eng, fn)

    def sync_wait(waits, eng="sync"):
        def fn(e, waits=tuple(waits)):
            mk_waits(e, waits)
        emit(eng, fn)

    Gap = outsb[:, 0, 0:384]
    tmp = [outsb[:, 0, 384:512], outsb[:, 0, 640:768]]
    sqr = [outsb[:, 0, 512:640], outsb[:, 0, 768:896]]

    def g_iota(e):
        e.iota(Gap, [[-1, 384]], base=128, channel_multiplier=1,
               allow_small_or_imprecise_dtypes=True)
        e.drain().then_inc(S["gp"], 1)
    bump("gp", "iota")

    w_pieces = [
        (WinS[:, :, 0:512], wint[:, :, 0:512], "w_win"),
        (WinS[:, :, 512:1024], wint[:, :, 512:1024], "w_win"),
        (W1s[:, 0:8, :, :], w1[:, 0:8, :, :], "w_w1a"),
        (WoutS[:, :, :], woutt, "w_wout"),
        (W1s[:, 8:16, :, :], w1[:, 8:16, :, :], "w_w1a"),
        (W2s[:, 0:8, :], w2[:, 0:8, :], "w_w2a"),
        (W1s[:, 16:24, :, :], w1[:, 16:24, :, :], "w_w1b"),
        (W2s[:, 8:16, :], w2[:, 8:16, :], "w_w2a"),
        (W1s[:, 24:32, :, :], w1[:, 24:32, :, :], "w_w1b"),
        (W2s[:, 16:24, :], w2[:, 16:24, :], "w_w2b"),
        (W2s[:, 24:32, :], w2[:, 24:32, :], "w_w2b"),
    ]
    emit("gpsimd", g_iota)
    for o_ap, i_ap, sem in w_pieces:
        dma("gpsimd", [], o_ap, i_ap, sem)

    def ln_bn(t):
        par = t % 2
        waits = [("xin", 16 * (t + 1)), wt("sc", ("rstd", t - 2))]

        def fn(e, par=par, xpar=t % 3):
            mv = mvs[:, par, :]
            e.bn_stats(stat[:, par, 0:6], xln[:, xpar, 0:512])
            e.bn_stats(stat[:, par, 6:12], xln[:, xpar, 512:1024])
            e.drain()
            e.bn_aggr(mv[:, 0:2], stat[:, par, 0:12])
            e.drain()
            return e.tensor_scalar(mv[:, 2:3], mv[:, 1:2], 1.0, EPS,
                                   OP.mult, OP.add)
        v_op(waits, fn, ("bn", t))

    def ln_rstd(t):
        par = t % 2

        def f1(e, par=par):
            e.activation(mvs[:, par, 3:4], mvs[:, par, 2:3],
                         AF.Ln, scale=1.0)
            e.drain()
            return e.activation(mvs[:, par, 4:5], mvs[:, par, 3:4],
                                AF.Exp, scale=-0.5)
        s_op([wt_req("vs", ("bn", t))], f1, ("rstd", t))

    def ln_xn(t):
        par = t % 2
        waits = [wt_req("sc", ("rstd", t)), wt("pe", ("T", t - 2))]

        def fn(e, par=par, xpar=t % 3):
            return e.tensor_scalar(xnb[:, par, :], xln[:, xpar, :],
                                   mvs[:, par, 0:1], mvs[:, par, 4:5],
                                   OP.subtract, OP.mult)
        v_op(waits, fn, ("xn", t))

    def pe_T(t, feat_free_wait, extra=()):
        par = t % 2
        mms = []
        for c in range(8):
            mms.append((feat[:, c * 128:(c + 1) * 128],
                        xnb[:, par, c * 128:(c + 1) * 128],
                        ident[:, :], True, True))
        t_mms([wt_req("vs", ("xn", t)), feat_free_wait,
               wt_req("vs", "ident")] + list(extra), mms, ("T", t))

    def v_evacT(t, buf, pos):
        waits = [wt_req("pe", ("T", t))]
        src = feat[:, :].rearrange("p (c t) -> p c t", t=128)
        dst = xnt[:, buf, :, pos * 128:pos * 128 + 128]

        def fn(e, src=src, dst=dst):
            return e.tensor_copy(dst, src)
        v_op(waits, fn, ("evacT", t))

    def pe_v(t, buf, pos):
        for half in range(2):
            mms = []
            for dc in range(8):
                mms.append((feat[:, half * 512:(half + 1) * 512],
                            xnt[:, buf, dc, pos * 128:pos * 128 + 128],
                            WinS[:, dc, half * 512:(half + 1) * 512],
                            dc == 0, dc == 7))
            t_mms([wt_req("vs", ("evacT", t)), ("w_win", 16 * (half + 1))],
                  mms, ("v", t) if half == 1 else None)

    def v_evacV(t):
        waits = [wt_req("pe", ("v", t)), wt("pe", ("attn", t - 3))]

        def fn(e, t=t):
            return e.tensor_copy(vring[:, t % 4, :], feat[:, :])
        v_op(waits, fn, ("evacV", t))

    def pe_attn(t, extra=()):
        po = outA if t % 2 == 1 else outB
        mms = []
        for h in range(8):
            hc = h % 4
            for di, dlt in enumerate((-1, 0, 1)):
                kt = t + dlt
                idx = hc * 3 + di
                mms.append((po[:, h * 128:(h + 1) * 128],
                            vring[:, kt % 4, h * 128:(h + 1) * 128],
                            Pt[:, idx, :], di == 0, di == 2))
        waits = [wt_req("sc", "pgen"), wt("vs", ("out_tt", t - 2)),
                 wt_req("vs", ("evacV", t + 1))]
        t_mms(waits + list(extra), mms, ("attn", t))

    def v_evacA(t):
        po = outA if t % 2 == 1 else outB
        waits = [wt_req("pe", ("attn", t)), wt("pe", ("op", t - 2))]

        def fn(e, t=t, po=po):
            return e.tensor_copy(attb[:, t % 2, :], po[:, :])
        v_op(waits, fn, ("evacA", t))

    def pe_outproj(t, extra=()):
        po = outA if t % 2 == 1 else outB
        mms = []
        for half in range(2):
            for dc in range(8):
                mms.append((po[:, half * 512:(half + 1) * 512],
                            attb[:, t % 2, dc * 128:(dc + 1) * 128],
                            WoutS[:, dc, half * 512:(half + 1) * 512],
                            dc == 0, False))
        waits = [wt_req("vs", ("evacA", t)), ("w_wout", 16)]
        t_mms(waits + list(extra), mms, ("op", t))

    def pe_f1pair(m, p, extra=()):
        bank = f1p[p % 2]
        buf = (m - 1) % 2 if m >= 1 else 1
        mms = []
        for i in range(2):
            wc = 2 * p + i
            for dc in range(8):
                mms.append((bank[:, i * 256:(i + 1) * 256],
                            W1s[:, wc, dc, :],
                            xnt[:, buf, dc, :],
                            dc == 0, dc == 7))
        waits = list(extra) + [("w_w1a", 16 * (1 if p < 4 else 2)) if p < 8
                               else ("w_w1b", 16 * (1 if p < 12 else 2))]
        if p >= 2:
            waits.append(wt("sc", ("h1", m, p - 2)))
        else:
            waits.append(wt("sc", ("h1", m - 1, p + 14)))
        waits.append(wt("vs", ("evacT", 2 * m + 2)))
        t_mms(waits, mms, ("f1", m, p))

    def s_h1pair(m, p, gchunk0):
        waits = [wt_req("pe", ("f1", m, p)), ("w_b1", 16)]
        for i in range(2):
            g = gchunk0 + i
            if g - 8 >= 0:
                pm, pk = divmod(g - 8, 32)
                waits.append(wt_req("pe", ("f2", pm, pk // 2)))

        def mk(i):
            wc = 2 * p + i
            g = gchunk0 + i

            def fn(e, wc=wc, g=g, p=p):
                return e.activation(
                    h1r[:, g % 8, :],
                    f1p[p % 2][:, (wc % 2) * 256:(wc % 2) * 256 + 256],
                    AF.Relu, bias=b1s[:, wc:wc + 1], scale=1.0)
            return fn
        s_op(waits, mk(0))
        s_op([], mk(1), ("h1", m, p))

    def pe_f2blk(m, k, gchunk0, a, b, extra=()):
        mms = []
        for i in range(2):
            wc = 2 * k + i
            g = gchunk0 + i
            for t in (a, b):
                po = outA if t % 2 == 1 else outB
                pos = 0 if t == a else 1
                for half in range(2):
                    stop = (k == 15) and (i == 1)
                    mms.append((po[:, half * 512:(half + 1) * 512],
                                h1r[:, g % 8, pos * 128:pos * 128 + 128],
                                W2s[:, wc, half * 512:(half + 1) * 512],
                                False, stop))
        t_mms([wt_req("sc", ("h1", m, k))] + list(extra), mms, ("f2", m, k))

    def v_out_tt(t, m):
        waits = [wt_req("pe", ("f2", m, 15)), ("w_b2", 16)]
        if t - 2 >= 1:
            waits.append(("sto", 16 * (t - 2)))
        po = outA if t % 2 == 1 else outB

        def fn(e, t=t, po=po):
            return e.scalar_tensor_tensor(outsb[:, t % 2, :], po[:, :], 0.0,
                                          b2s[:, :], OP.add, OP.add)
        v_op(waits, fn, ("out_tt", t))

    def sync_load(t):
        waits = [wt_req("vs", ("xn", t - 3))] if t - 3 >= 0 else []
        if 1 <= t - 3 <= NMAIN:
            waits.append(("pre", 16 * (t - 3)))
        dma("sync", waits, xln[:, t % 3, :], xs[t * 128:(t + 1) * 128, :], "xin")

    def sync_prestore(t):
        dma("sync", [("xin", 16 * (t + 1))],
            outh[(t - 1) * 128:t * 128, :], xln[:, t % 3, :], "pre")

    def g_accum_store(t):
        dma("gpsimd", [wt_req("vs", ("out_tt", t)), ("pre", 16 * t)],
            outh[(t - 1) * 128:t * 128, :], outsb[:, t % 2, :], "sto",
            accum=True)

    for t in (0, 1, 2):
        dma("sync", [], xln[:, t % 3, :], xs[t * 128:(t + 1) * 128, :], "xin")
    dma("sync", [], b1s[:, :], b1t, "w_b1", "done")
    dma("sync", [], b2s[:, :], b2bc, "w_b2", "done")

    def s_warm(e):
        e.activation(mvs[:, 0, 5:6], mvs[:, 0, 5:6], AF.Ln, scale=1.0)
    emit("scalar", s_warm)

    ln_bn(0)

    def mk_ident(e):
        return e.tensor_scalar(ident[:, :], outsb[:, 0, 128:256],
                               0.0, None, OP.is_equal)
    v_op([wt_req("gp", "iota")], mk_ident, "ident")

    def mk_lninv(e):
        for hc in range(4):
            val = math.log(1.0 / (STD[hc] * math.sqrt(2.0 * math.pi)))
            hi = float(np.float32(val))
            lo = val - hi
            e.memset(lninv[:, hc:hc + 1], hi)
            e.drain()
            e.tensor_scalar(lninv[:, hc:hc + 1], lninv[:, hc:hc + 1],
                            lo, None, OP.add)
        return None
    v_op([], mk_lninv, "lninv")
    ln_rstd(0)
    ln_xn(0)

    def psq_round(j):
        hc, di = divmod(j, 3)
        lo, hi = {0: (256, 384), 1: (128, 256), 2: (0, 128)}[di]
        par = j % 2

        def fv(e, hc=hc, lo=lo, hi=hi, par=par):
            e.tensor_scalar(tmp[par], outsb[:, 0, lo:hi],
                            OFF[hc], 1.0 / STD[hc], OP.subtract, OP.mult)
            e.drain()
            return e.tensor_tensor(sqr[par], tmp[par], tmp[par], OP.mult)
        waits = [wt_req("gp", "iota")]
        if j >= 2:
            waits.append(wt_req("sc", ("pexp", j - 2)))
        v_op(waits, fv, ("psq", j))

        def fs(e, j=j, par=par, hc=hc):
            return e.activation(Pt[:, j, :], sqr[par], AF.Exp,
                                bias=lninv[:, hc:hc + 1], scale=-0.5)
        s_op([wt_req("vs", ("psq", j)), wt_req("vs", "lninv")],
             fs, ("pexp", j))

    psq_round(0)
    psq_round(1)

    sync_wait([("xin", 32)])
    sync_prestore(1)
    ln_bn(1)
    ln_rstd(1)
    ln_xn(1)
    psq_round(2)
    psq_round(3)

    pe_T(0, None)
    v_evacT(0, 0, 0)
    pe_v(0, 0, 0)
    v_evacV(0)

    sync_wait([("xin", 48)])
    sync_prestore(2)
    ln_bn(2)
    ln_rstd(2)
    ln_xn(2)
    psq_round(4)
    psq_round(5)

    pe_T(1, wt_req("vs", ("evacV", 0)))
    v_evacT(1, 1, 0)
    pe_v(1, 1, 0)
    v_evacV(1)

    sync_load(3)
    sync_wait([("xin", 64)])
    sync_prestore(3)
    ln_bn(3)
    ln_rstd(3)
    ln_xn(3)
    psq_round(6)
    psq_round(7)

    pe_T(2, wt_req("vs", ("evacV", 1)))
    v_evacT(2, 1, 1)
    pe_v(2, 1, 1)
    v_evacV(2)

    sync_load(4)
    sync_wait([("xin", 80)])
    sync_prestore(4)
    ln_bn(4)
    ln_rstd(4)
    ln_xn(4)
    for _j in (8, 9, 10, 11):
        psq_round(_j)
    after[("sc", "pgen")] = after[("sc", ("pexp", 11))]

    gchunk = 0
    for m in range(n_macros):
        a, b, a2, b2 = 2 * m + 1, 2 * m + 2, 2 * m + 3, 2 * m + 4
        buf = m % 2
        t5, t6 = 2 * m + 5, 2 * m + 6
        g0 = gchunk

        if t5 < NT:
            sync_load(t5)
            if t5 <= NMAIN:
                sync_wait([("xin", 16 * (t5 + 1))])
                sync_prestore(t5)
        if t6 < NT:
            sync_load(t6)
            if t6 <= NMAIN:
                sync_wait([("xin", 16 * (t6 + 1))])
                sync_prestore(t6)

        def tail_blk(k):
            extra = []
            if m == 0:
                if k == 0:
                    extra.append(("w_w2a", 16))
                elif k == 4:
                    extra.append(("w_w2a", 32))
                elif k == 8:
                    extra.append(("w_w2b", 16))
            pe_f2blk(m, k, g0 + 2 * k, a, b, extra=extra)

        def defer_blk(k):
            extra = [("w_w2b", 32)] if m == 1 else []
            pe_f2blk(m - 1, k, (m - 1) * 32 + 2 * k, a - 2, b - 2,
                     extra=extra)

        if m == 0:
            pe_T(a2, wt_req("vs", ("evacV", b)))
            v_evacT(a2, buf, 0)
            pe_v(a2, buf, 0)
            v_evacV(a2)
            pe_attn(a)
            v_evacA(a)
            pe_attn(b)
            v_evacA(b)
            ln_bn(t5)
            ln_rstd(t5)
            ln_xn(t5)
            pe_T(b2, wt_req("vs", ("evacV", a2)))
            v_evacT(b2, buf, 1)
            pe_v(b2, buf, 1)
            v_evacV(b2)
            for p in range(3):
                pe_f1pair(m, p)
                s_h1pair(m, p, g0 + 2 * p)
            pe_outproj(a)
            pe_outproj(b)
            tail_blk(0)
            tail_blk(1)
            pe_f1pair(m, 3)
            s_h1pair(m, 3, g0 + 6)
            pe_f1pair(m, 4)
            s_h1pair(m, 4, g0 + 8)
            for k in range(2, 12):
                tail_blk(k)
                p = k + 3
                pe_f1pair(m, p)
                s_h1pair(m, p, g0 + 2 * p)
                if k == 6 and t6 < NT:
                    ln_bn(t6)
                    ln_rstd(t6)
                    ln_xn(t6)
            pe_f1pair(m, 15)
            s_h1pair(m, 15, g0 + 30)
        else:
            pe_T(a2, wt_req("vs", ("evacV", b)))
            v_evacT(a2, buf, 0)
            defer_blk(12)
            pe_f1pair(m, 0)
            s_h1pair(m, 0, g0 + 0)
            defer_blk(13)
            pe_f1pair(m, 1)
            s_h1pair(m, 1, g0 + 2)
            pe_v(a2, buf, 0)
            v_evacV(a2)
            defer_blk(14)
            defer_blk(15)
            v_out_tt(a - 2, m - 1)
            g_accum_store(a - 2)
            v_out_tt(b - 2, m - 1)
            g_accum_store(b - 2)
            pe_f1pair(m, 2)
            s_h1pair(m, 2, g0 + 4)
            pe_attn(a)
            v_evacA(a)
            pe_f1pair(m, 3)
            s_h1pair(m, 3, g0 + 6)
            pe_attn(b)
            v_evacA(b)
            if t5 < NT:
                ln_bn(t5)
                ln_rstd(t5)
                ln_xn(t5)
            if b2 < NT:
                pe_T(b2, wt_req("vs", ("evacV", a2)))
                v_evacT(b2, buf, 1)
            pe_outproj(a)
            pe_outproj(b)
            tail_blk(0)
            tail_blk(1)
            pe_f1pair(m, 4)
            s_h1pair(m, 4, g0 + 8)
            pe_f1pair(m, 5)
            s_h1pair(m, 5, g0 + 10)
            if b2 < NT:
                pe_v(b2, buf, 1)
                v_evacV(b2)
            for k in range(2, 12):
                tail_blk(k)
                p = k + 4
                pe_f1pair(m, p)
                s_h1pair(m, p, g0 + 2 * p)
                if k == 6 and t6 < NT:
                    ln_bn(t6)
                    ln_rstd(t6)
                    ln_xn(t6)

        gchunk += 32

    if n_macros:
        ml = n_macros - 1
        tb = 2 * n_macros
        for k in range(12, 16):
            pe_f2blk(ml, k, ml * 32 + 2 * k, tb - 1, tb)
        v_out_tt(tb - 1, ml)
        g_accum_store(tb - 1)
        v_out_tt(tb, ml)
        g_accum_store(tb)

    tail = [("sto", 16 * cnt["sto"] // 16)] if cnt["sto"] else []
    tail = [("sto", cnt["sto"])] if cnt["sto"] else []
    for n in ("xin", "pre", "w_win", "w_w1a", "w_w1b", "w_wout",
              "w_w2a", "w_w2b", "w_b1", "w_b2"):
        if cnt[n]:
            tail.append((n, cnt[n]))
    sync_wait(tail)

    if debug_dump:
        dbf = nc.dram_tensor("dbg_bf", [128, 9088], BF16, kind="ExternalOutput").ap()
        dff = nc.dram_tensor("dbg_f32", [128, 26], F32, kind="ExternalOutput").ap()
        pieces_bf = [(ident[:, :], 128), (Pt[:, :, :].rearrange("p a b -> p (a b)"), 1536),
                     (xnb[:, :, :].rearrange("p a b -> p (a b)"), 2048),
                     (vring[:, 0, :], 1024), (attb[:, 0, :], 1024),
                     (h1r[:, 0, :], 256), (WinS[:, 0, 0:1024], 1024),
                     (xnt[:, 0, 0, :], 256), (xnt[:, 1, 0, :], 256),
                     (W1s[:, 0, 0, :], 128), (W2s[:, 0, 0:256], 256),
                     (WoutS[:, 0, 0:256], 256),
                     ]
        off = 0
        k = cnt["sto"]
        for ap, w in pieces_bf:
            def fn(e, ap=ap, o=off, w=w):
                e.dma_start(out=dbf[:, o:o + w], in_=ap).then_inc(S["sto"], 16)
            emit("sync", fn)
            off += w
            k += 16
        def fn2(e):
            e.dma_start(out=dff[:, 0:20],
                        in_=mvs[:, :, :].rearrange("p a b -> p (a b)")).then_inc(S["sto"], 16)
        emit("sync", fn2)
        k += 16
        def fn3(e, k=k):
            e.dma_start(out=dff[:, 20:26], in_=stat[:, 0, 0:6]).then_inc(S["sto"], 16)
            e.wait_ge(S["sto"], k + 16)
        emit("sync", fn3)

    with nc.Block() as init:
        @init.sync
        def _(e):
            for n in sem_names:
                e.sem_clear(S[n])

    with nc.Block(no_gpsimd_drain=True) as blk:
        @blk.tensor
        def _(e):
            for fn in ops["tensor"]:
                fn(e)

        @blk.vector
        def _(e):
            for fn in ops["vector"]:
                fn(e)

        @blk.scalar
        def _(e):
            for fn in ops["scalar"]:
                fn(e)

        @blk.sync
        def _(e):
            for fn in ops["sync"]:
                fn(e)

        @blk.gpsimd
        def _(e):
            for fn in ops["gpsimd"]:
                fn(e)

    return nc


_NC_CACHE = None


def kernel(L0_attn_values, L0_attn_input_weights, L0_attn_output_weight,
           L0_ffn_w1, L0_ffn_b1, L0_ffn_w2, L0_ffn_b2):
    global _NC_CACHE
    x = np.ascontiguousarray(np.asarray(L0_attn_values, dtype=np.float32))
    B, L, Dm = x.shape
    assert (B, L, Dm) == (4, 4096, 1024)

    wint = np.ascontiguousarray(
        np.asarray(L0_attn_input_weights, np.float32).T
        .reshape(8, 128, Dm).transpose(1, 0, 2))
    woutt = np.ascontiguousarray(
        np.asarray(L0_attn_output_weight, np.float32).T
        .reshape(8, 128, Dm).transpose(1, 0, 2))
    w1 = np.ascontiguousarray(
        np.asarray(L0_ffn_w1, np.float32)
        .reshape(8, 128, 32, 128).transpose(1, 2, 0, 3))
    w2 = np.ascontiguousarray(
        np.asarray(L0_ffn_w2, np.float32)
        .reshape(32, 128, Dm).transpose(1, 0, 2))
    b1t = np.ascontiguousarray(np.asarray(L0_ffn_b1, np.float32).reshape(32, 128).T)
    b2bc = np.ascontiguousarray(np.broadcast_to(
        np.asarray(L0_ffn_b2, np.float32).reshape(1, Dm), (128, Dm)))

    if _NC_CACHE is None:
        _NC_CACHE = build_nc()
    nc = _NC_CACHE

    in_maps = []
    for c in range(8):
        bidx, half = divmod(c, 2)
        s = half * 2048
        lo, hi = s - 128, s + 2048 + 128
        slab = np.zeros((NT * 128, Dm), dtype=np.float32)
        clo, chi = max(lo, 0), min(hi, L)
        slab[clo - lo:chi - lo, :] = x[bidx, clo:chi, :]
        in_maps.append({
            "xs": slab, "wint": wint, "woutt": woutt,
            "w1": w1, "w2": w2, "b1t": b1t, "b2bc": b2bc,
        })

    res = run_bass_kernel_spmd(nc, in_maps, core_ids=list(range(8)))

    out = np.empty((4, 4096, 1024), dtype=np.float32)
    for c in range(8):
        bidx, half = divmod(c, 2)
        s = half * 2048
        out[bidx, s:s + 2048, :] = np.asarray(res.results[c]["out"],
                                              dtype=np.float32)
    return out


# version 20
# speedup vs baseline: 1.0038x; 1.0038x over previous
import math
import numpy as np
import concourse.bass as bass
import concourse.mybir as mybir
from concourse.bass_utils import run_bass_kernel_spmd

F32 = mybir.dt.float32
BF16 = mybir.dt.bfloat16
AF = mybir.ActivationFunctionType
OP = mybir.AluOpType

D = 1024
NT = 18
NMAIN = 16
NMACRO = 8
STD = [1.0, 2.0, 4.0, 8.0]
OFF = [-1.0, -2.0, -4.0, -8.0]
EPS = 1e-5


def build_nc(n_macros=NMACRO, debug_dump=False, scratch=8192):
    nc = bass.Bass("TRN2", target_bir_lowering=False,
                   dynamic_dma_scratch_size=scratch,
                   detect_race_conditions=False)

    xs = nc.dram_tensor("xs", [NT * 128, D], F32, kind="ExternalInput").ap()
    wint = nc.dram_tensor("wint", [128, 8, D], F32, kind="ExternalInput").ap()
    woutt = nc.dram_tensor("woutt", [128, 8, D], F32, kind="ExternalInput").ap()
    w1 = nc.dram_tensor("w1", [128, 32, 8, 128], F32, kind="ExternalInput").ap()
    w2 = nc.dram_tensor("w2", [128, 32, D], F32, kind="ExternalInput").ap()
    b1t = nc.dram_tensor("b1t", [128, 32], F32, kind="ExternalInput").ap()
    b2bc = nc.dram_tensor("b2bc", [128, D], F32, kind="ExternalInput").ap()
    outh = nc.dram_tensor("out", [NMAIN * 128, D], F32, kind="ExternalOutput").ap()

    W1s = nc.alloc_sbuf_tensor("W1s", [128, 32, 8, 128], BF16)
    W2s = nc.alloc_sbuf_tensor("W2s", [128, 32, 1024], BF16)
    WinS = nc.alloc_sbuf_tensor("WinS", [128, 8, 1024], BF16)
    WoutS = nc.alloc_sbuf_tensor("WoutS", [128, 8, 1024], BF16)
    xln = nc.alloc_sbuf_tensor("xln", [128, 3, D], F32)
    outsb = nc.alloc_sbuf_tensor("outsb", [128, 2, D], F32)
    xnb = nc.alloc_sbuf_tensor("xnb", [128, 2, D], BF16)
    xnt = nc.alloc_sbuf_tensor("xnt", [128, 2, 8, 256], BF16)
    vring = nc.alloc_sbuf_tensor("vring", [128, 4, D], BF16)
    attb = nc.alloc_sbuf_tensor("attb", [128, 2, D], BF16)
    h1r = nc.alloc_sbuf_tensor("h1r", [128, 8, 256], BF16)
    Pt = nc.alloc_sbuf_tensor("Pt", [128, 12, 128], BF16)
    ident = nc.alloc_sbuf_tensor("ident", [128, 128], BF16)
    b1s = nc.alloc_sbuf_tensor("b1s", [128, 32], F32)
    b2s = nc.alloc_sbuf_tensor("b2s", [128, D], F32)
    mvs = nc.alloc_sbuf_tensor("mvs", [128, 2, 10], F32)
    stat = nc.alloc_sbuf_tensor("stat", [128, 2, 12], F32)
    lninv = nc.alloc_sbuf_tensor("lninv", [128, 4], F32)

    feat = nc.alloc_psum_tensor("feat", [128, 1024], F32)
    outA = nc.alloc_psum_tensor("outA", [128, 1024], F32)
    outB = nc.alloc_psum_tensor("outB", [128, 1024], F32)
    f1p = [nc.alloc_psum_tensor("f1a", [128, 512], F32),
           nc.alloc_psum_tensor("f1b", [128, 512], F32)]

    sem_names = ["pe", "vs", "sc", "gp", "xin", "pre", "sto",
                 "w_win", "w_w1a", "w_w1b", "w_wout", "w_w2a", "w_w2b",
                 "w_b1", "w_b2"]
    S = {n: nc.alloc_semaphore("sem_" + n) for n in sem_names}

    cnt = {n: 0 for n in sem_names}
    after = {}
    ops = {k: [] for k in ("tensor", "vector", "scalar", "sync", "gpsimd")}

    def emit(eng, fn):
        ops[eng].append(fn)

    def bump(sem, event=None, n=1):
        cnt[sem] += n
        if event is not None:
            assert (sem, event) not in after, event
            after[(sem, event)] = cnt[sem]
        return cnt[sem]

    def wt(sem, event):
        v = after.get((sem, event))
        return None if v is None else (sem, v)

    def wt_req(sem, event):
        v = after[(sem, event)]
        return (sem, v)

    def mk_waits(e, waits):
        for w in waits:
            if w is None:
                continue
            sn, val = w
            e.wait_ge(S[sn], val)

    def t_mms(waits, mms, event=None):
        do_inc = event is not None
        if do_inc:
            bump("pe", event)

        def fn(e, waits=tuple(waits), mms=tuple(mms), do_inc=do_inc):
            mk_waits(e, waits)
            last = None
            for (o, l, r, st, sp) in mms:
                last = e.matmul(o, l, r, start=st, stop=sp,
                                skip_group_check=True)
            if do_inc:
                last.then_inc(S["pe"], 1)
        emit("tensor", fn)

    def v_op(waits, opfn, event=None):
        do_inc = event is not None
        if do_inc:
            bump("vs", event)

        def fn(e, waits=tuple(waits), opfn=opfn, do_inc=do_inc):
            mk_waits(e, waits)
            ins = opfn(e)
            if do_inc:
                if ins is None:
                    ins = e.drain()
                ins.then_inc(S["vs"], 1)
        emit("vector", fn)

    def s_op(waits, opfn, event=None):
        do_inc = event is not None
        if do_inc:
            bump("sc", event)

        def fn(e, waits=tuple(waits), opfn=opfn, do_inc=do_inc):
            mk_waits(e, waits)
            ins = opfn(e)
            if do_inc:
                if ins is None:
                    ins = e.drain()
                ins.then_inc(S["sc"], 1)
        emit("scalar", fn)

    def dma(eng, waits, out_ap, in_ap, sem, event=None, accum=False):
        bump(sem, event, 16)

        def fn(e, waits=tuple(waits), o=out_ap, i=in_ap, sem=sem, accum=accum):
            mk_waits(e, waits)
            if accum:
                e.dma_start(out=o, in_=i, accum_op=OP.add).then_inc(S[sem], 16)
            else:
                e.dma_start(out=o, in_=i).then_inc(S[sem], 16)
        emit(eng, fn)

    def sync_wait(waits, eng="sync"):
        def fn(e, waits=tuple(waits)):
            mk_waits(e, waits)
        emit(eng, fn)

    Gap = outsb[:, 0, 0:384]
    tmp = [outsb[:, 0, 384:512], outsb[:, 0, 640:768]]
    sqr = [outsb[:, 0, 512:640], outsb[:, 0, 768:896]]

    def g_iota(e):
        e.iota(Gap, [[-1, 384]], base=128, channel_multiplier=1,
               allow_small_or_imprecise_dtypes=True)
        e.drain().then_inc(S["gp"], 1)
    bump("gp", "iota")

    w_pieces = [
        (WinS[:, :, 0:512], wint[:, :, 0:512], "w_win"),
        (WinS[:, :, 512:1024], wint[:, :, 512:1024], "w_win"),
        (W1s[:, 0:8, :, :], w1[:, 0:8, :, :], "w_w1a"),
        (WoutS[:, :, :], woutt, "w_wout"),
        (W1s[:, 8:16, :, :], w1[:, 8:16, :, :], "w_w1a"),
        (W2s[:, 0:8, :], w2[:, 0:8, :], "w_w2a"),
        (W1s[:, 16:24, :, :], w1[:, 16:24, :, :], "w_w1b"),
        (W2s[:, 8:16, :], w2[:, 8:16, :], "w_w2a"),
        (W1s[:, 24:32, :, :], w1[:, 24:32, :, :], "w_w1b"),
        (W2s[:, 16:24, :], w2[:, 16:24, :], "w_w2b"),
        (W2s[:, 24:32, :], w2[:, 24:32, :], "w_w2b"),
    ]
    emit("gpsimd", g_iota)
    for o_ap, i_ap, sem in w_pieces:
        dma("gpsimd", [], o_ap, i_ap, sem)

    def ln_bn(t):
        par = t % 2
        waits = [("xin", 16 * (t + 1)), wt("sc", ("rstd", t - 2))]

        def fn(e, par=par, xpar=t % 3):
            mv = mvs[:, par, :]
            e.bn_stats(stat[:, par, 0:6], xln[:, xpar, 0:512])
            e.bn_stats(stat[:, par, 6:12], xln[:, xpar, 512:1024])
            e.drain()
            e.bn_aggr(mv[:, 0:2], stat[:, par, 0:12])
            e.drain()
            return e.tensor_scalar(mv[:, 2:3], mv[:, 1:2], 1.0, EPS,
                                   OP.mult, OP.add)
        v_op(waits, fn, ("bn", t))

    def ln_rstd(t):
        par = t % 2

        def f1(e, par=par):
            e.activation(mvs[:, par, 3:4], mvs[:, par, 2:3],
                         AF.Ln, scale=1.0)
            e.drain()
            return e.activation(mvs[:, par, 4:5], mvs[:, par, 3:4],
                                AF.Exp, scale=-0.5)
        s_op([wt_req("vs", ("bn", t))], f1, ("rstd", t))

    def ln_xn(t):
        par = t % 2
        waits = [wt_req("sc", ("rstd", t)), wt("pe", ("T", t - 2))]

        def fn(e, par=par, xpar=t % 3):
            return e.tensor_scalar(xnb[:, par, :], xln[:, xpar, :],
                                   mvs[:, par, 0:1], mvs[:, par, 4:5],
                                   OP.subtract, OP.mult)
        v_op(waits, fn, ("xn", t))

    def pe_T(t, feat_free_wait, extra=()):
        par = t % 2
        mms = []
        for c in range(8):
            mms.append((feat[:, c * 128:(c + 1) * 128],
                        xnb[:, par, c * 128:(c + 1) * 128],
                        ident[:, :], True, True))
        t_mms([wt_req("vs", ("xn", t)), feat_free_wait,
               wt_req("vs", "ident")] + list(extra), mms, ("T", t))

    def v_evacT(t, buf, pos):
        waits = [wt_req("pe", ("T", t))]
        src = feat[:, :].rearrange("p (c t) -> p c t", t=128)
        dst = xnt[:, buf, :, pos * 128:pos * 128 + 128]

        def fn(e, src=src, dst=dst):
            return e.tensor_copy(dst, src)
        v_op(waits, fn, ("evacT", t))

    def pe_v(t, buf, pos):
        for half in range(2):
            mms = []
            for dc in range(8):
                mms.append((feat[:, half * 512:(half + 1) * 512],
                            xnt[:, buf, dc, pos * 128:pos * 128 + 128],
                            WinS[:, dc, half * 512:(half + 1) * 512],
                            dc == 0, dc == 7))
            t_mms([wt_req("vs", ("evacT", t)), ("w_win", 16 * (half + 1))],
                  mms, ("v", t) if half == 1 else None)

    def v_evacV(t):
        waits = [wt_req("pe", ("v", t)), wt("pe", ("attn", t - 3))]

        def fn(e, t=t):
            return e.tensor_copy(vring[:, t % 4, :], feat[:, :])
        v_op(waits, fn, ("evacV", t))

    def pe_attn(t, extra=()):
        po = outA if t % 2 == 1 else outB
        mms = []
        for h in range(8):
            hc = h % 4
            for di, dlt in enumerate((-1, 0, 1)):
                kt = t + dlt
                idx = hc * 3 + di
                mms.append((po[:, h * 128:(h + 1) * 128],
                            vring[:, kt % 4, h * 128:(h + 1) * 128],
                            Pt[:, idx, :], di == 0, di == 2))
        waits = [wt_req("sc", "pgen"), wt("vs", ("out_tt", t - 2)),
                 wt_req("vs", ("evacV", t + 1))]
        t_mms(waits + list(extra), mms, ("attn", t))

    def v_evacA(t):
        po = outA if t % 2 == 1 else outB
        waits = [wt_req("pe", ("attn", t)), wt("pe", ("op", t - 2))]

        def fn(e, t=t, po=po):
            return e.tensor_copy(attb[:, t % 2, :], po[:, :])
        v_op(waits, fn, ("evacA", t))

    def pe_outproj(t, extra=()):
        po = outA if t % 2 == 1 else outB
        mms = []
        for half in range(2):
            for dc in range(8):
                mms.append((po[:, half * 512:(half + 1) * 512],
                            attb[:, t % 2, dc * 128:(dc + 1) * 128],
                            WoutS[:, dc, half * 512:(half + 1) * 512],
                            dc == 0, False))
        waits = [wt_req("vs", ("evacA", t)), ("w_wout", 16)]
        t_mms(waits + list(extra), mms, ("op", t))

    def pe_f1pair(m, p, extra=()):
        bank = f1p[p % 2]
        buf = (m - 1) % 2 if m >= 1 else 1
        mms = []
        for i in range(2):
            wc = 2 * p + i
            for dc in range(8):
                mms.append((bank[:, i * 256:(i + 1) * 256],
                            W1s[:, wc, dc, :],
                            xnt[:, buf, dc, :],
                            dc == 0, dc == 7))
        waits = list(extra) + [("w_w1a", 16 * (1 if p < 4 else 2)) if p < 8
                               else ("w_w1b", 16 * (1 if p < 12 else 2))]
        if p >= 2:
            waits.append(wt("sc", ("h1", m, p - 2)))
        else:
            waits.append(wt("sc", ("h1", m - 1, p + 14)))
        waits.append(wt("vs", ("evacT", 2 * m + 2)))
        t_mms(waits, mms, ("f1", m, p))

    def s_h1pair(m, p, gchunk0):
        waits = [wt_req("pe", ("f1", m, p)), ("w_b1", 16)]
        for i in range(2):
            g = gchunk0 + i
            if g - 8 >= 0:
                pm, pk = divmod(g - 8, 32)
                waits.append(wt_req("pe", ("f2", pm, pk // 2)))

        def mk(i):
            wc = 2 * p + i
            g = gchunk0 + i

            def fn(e, wc=wc, g=g, p=p):
                return e.activation(
                    h1r[:, g % 8, :],
                    f1p[p % 2][:, (wc % 2) * 256:(wc % 2) * 256 + 256],
                    AF.Relu, bias=b1s[:, wc:wc + 1], scale=1.0)
            return fn
        s_op(waits, mk(0))
        s_op([], mk(1), ("h1", m, p))

    def pe_f2blk(m, k, gchunk0, a, b, extra=()):
        mms = []
        for i in range(2):
            wc = 2 * k + i
            g = gchunk0 + i
            for t in (a, b):
                po = outA if t % 2 == 1 else outB
                pos = 0 if t == a else 1
                for half in range(2):
                    stop = (k == 15) and (i == 1)
                    mms.append((po[:, half * 512:(half + 1) * 512],
                                h1r[:, g % 8, pos * 128:pos * 128 + 128],
                                W2s[:, wc, half * 512:(half + 1) * 512],
                                False, stop))
        t_mms([wt_req("sc", ("h1", m, k))] + list(extra), mms, ("f2", m, k))

    def v_out_tt(t, m):
        waits = [wt_req("pe", ("f2", m, 15)), ("w_b2", 16)]
        if t - 2 >= 1:
            waits.append(("sto", 16 * (t - 2)))
        po = outA if t % 2 == 1 else outB

        def fn(e, t=t, po=po):
            return e.scalar_tensor_tensor(outsb[:, t % 2, :], po[:, :], 0.0,
                                          b2s[:, :], OP.add, OP.add)
        v_op(waits, fn, ("out_tt", t))

    def sync_load(t):
        waits = [wt_req("vs", ("xn", t - 3))] if t - 3 >= 0 else []
        if 1 <= t - 3 <= NMAIN:
            waits.append(("pre", 16 * (t - 3)))
        dma("sync", waits, xln[:, t % 3, :], xs[t * 128:(t + 1) * 128, :], "xin")

    def sync_prestore(t):
        dma("sync", [("xin", 16 * (t + 1))],
            outh[(t - 1) * 128:t * 128, :], xln[:, t % 3, :], "pre")

    def g_accum_store(t):
        dma("gpsimd", [wt_req("vs", ("out_tt", t)), ("pre", 16 * t)],
            outh[(t - 1) * 128:t * 128, :], outsb[:, t % 2, :], "sto",
            accum=True)

    for t in (0, 1, 2):
        dma("sync", [], xln[:, t % 3, :], xs[t * 128:(t + 1) * 128, :], "xin")
    dma("sync", [], b1s[:, :], b1t, "w_b1", "done")
    dma("sync", [], b2s[:, :], b2bc, "w_b2", "done")

    def s_warm(e):
        e.activation(mvs[:, 0, 5:6], mvs[:, 0, 5:6], AF.Ln, scale=1.0)
    emit("scalar", s_warm)

    ln_bn(0)

    def mk_ident(e):
        return e.tensor_scalar(ident[:, :], outsb[:, 0, 128:256],
                               0.0, None, OP.is_equal)
    v_op([wt_req("gp", "iota")], mk_ident, "ident")

    def mk_lninv(e):
        for hc in range(4):
            val = math.log(1.0 / (STD[hc] * math.sqrt(2.0 * math.pi)))
            hi = float(np.float32(val))
            lo = val - hi
            e.memset(lninv[:, hc:hc + 1], hi)
            e.drain()
            e.tensor_scalar(lninv[:, hc:hc + 1], lninv[:, hc:hc + 1],
                            lo, None, OP.add)
        return None
    v_op([], mk_lninv, "lninv")
    ln_rstd(0)
    ln_xn(0)

    def psq_round(j):
        hc, di = divmod(j, 3)
        lo, hi = {0: (256, 384), 1: (128, 256), 2: (0, 128)}[di]
        par = j % 2

        def fv(e, hc=hc, lo=lo, hi=hi, par=par):
            e.tensor_scalar(tmp[par], outsb[:, 0, lo:hi],
                            OFF[hc], 1.0 / STD[hc], OP.subtract, OP.mult)
            e.drain()
            return e.tensor_tensor(sqr[par], tmp[par], tmp[par], OP.mult)
        waits = [wt_req("gp", "iota")]
        if j >= 2:
            waits.append(wt_req("sc", ("pexp", j - 2)))
        v_op(waits, fv, ("psq", j))

        def fs(e, j=j, par=par, hc=hc):
            return e.activation(Pt[:, j, :], sqr[par], AF.Exp,
                                bias=lninv[:, hc:hc + 1], scale=-0.5)
        s_op([wt_req("vs", ("psq", j)), wt_req("vs", "lninv")],
             fs, ("pexp", j))

    for _j in range(12):
        psq_round(_j)
    after[("sc", "pgen")] = after[("sc", ("pexp", 11))]

    sync_wait([("xin", 32)])
    sync_prestore(1)
    ln_bn(1)
    ln_rstd(1)
    ln_xn(1)

    pe_T(0, None)
    v_evacT(0, 0, 0)
    pe_v(0, 0, 0)
    v_evacV(0)

    sync_wait([("xin", 48)])
    sync_prestore(2)
    ln_bn(2)
    ln_rstd(2)
    ln_xn(2)

    pe_T(1, wt_req("vs", ("evacV", 0)))
    v_evacT(1, 1, 0)
    pe_v(1, 1, 0)
    v_evacV(1)

    sync_load(3)
    sync_wait([("xin", 64)])
    sync_prestore(3)
    ln_bn(3)
    ln_rstd(3)
    ln_xn(3)

    pe_T(2, wt_req("vs", ("evacV", 1)))
    v_evacT(2, 1, 1)
    pe_v(2, 1, 1)
    v_evacV(2)

    sync_load(4)
    sync_wait([("xin", 80)])
    sync_prestore(4)
    ln_bn(4)
    ln_rstd(4)
    ln_xn(4)

    gchunk = 0
    for m in range(n_macros):
        a, b, a2, b2 = 2 * m + 1, 2 * m + 2, 2 * m + 3, 2 * m + 4
        buf = m % 2
        t5, t6 = 2 * m + 5, 2 * m + 6
        g0 = gchunk

        if t5 < NT:
            sync_load(t5)
            if t5 <= NMAIN:
                sync_wait([("xin", 16 * (t5 + 1))])
                sync_prestore(t5)
        if t6 < NT:
            sync_load(t6)
            if t6 <= NMAIN:
                sync_wait([("xin", 16 * (t6 + 1))])
                sync_prestore(t6)

        def tail_blk(k):
            extra = []
            if m == 0:
                if k == 0:
                    extra.append(("w_w2a", 16))
                elif k == 4:
                    extra.append(("w_w2a", 32))
                elif k == 8:
                    extra.append(("w_w2b", 16))
            pe_f2blk(m, k, g0 + 2 * k, a, b, extra=extra)

        def defer_blk(k):
            extra = [("w_w2b", 32)] if m == 1 else []
            pe_f2blk(m - 1, k, (m - 1) * 32 + 2 * k, a - 2, b - 2,
                     extra=extra)

        if m == 0:
            pe_T(a2, wt_req("vs", ("evacV", b)))
            v_evacT(a2, buf, 0)
            pe_v(a2, buf, 0)
            v_evacV(a2)
            pe_attn(a)
            v_evacA(a)
            pe_attn(b)
            v_evacA(b)
            ln_bn(t5)
            ln_rstd(t5)
            ln_xn(t5)
            pe_T(b2, wt_req("vs", ("evacV", a2)))
            v_evacT(b2, buf, 1)
            pe_v(b2, buf, 1)
            v_evacV(b2)
            for p in range(3):
                pe_f1pair(m, p)
                s_h1pair(m, p, g0 + 2 * p)
            pe_outproj(a)
            pe_outproj(b)
            tail_blk(0)
            tail_blk(1)
            pe_f1pair(m, 3)
            s_h1pair(m, 3, g0 + 6)
            pe_f1pair(m, 4)
            s_h1pair(m, 4, g0 + 8)
            for k in range(2, 12):
                tail_blk(k)
                p = k + 3
                pe_f1pair(m, p)
                s_h1pair(m, p, g0 + 2 * p)
                if k == 6 and t6 < NT:
                    ln_bn(t6)
                    ln_rstd(t6)
                    ln_xn(t6)
            pe_f1pair(m, 15)
            s_h1pair(m, 15, g0 + 30)
        else:
            pe_T(a2, wt_req("vs", ("evacV", b)))
            v_evacT(a2, buf, 0)
            defer_blk(12)
            pe_f1pair(m, 0)
            s_h1pair(m, 0, g0 + 0)
            defer_blk(13)
            pe_f1pair(m, 1)
            s_h1pair(m, 1, g0 + 2)
            pe_v(a2, buf, 0)
            v_evacV(a2)
            defer_blk(14)
            defer_blk(15)
            v_out_tt(a - 2, m - 1)
            g_accum_store(a - 2)
            v_out_tt(b - 2, m - 1)
            g_accum_store(b - 2)
            pe_f1pair(m, 2)
            s_h1pair(m, 2, g0 + 4)
            pe_attn(a)
            v_evacA(a)
            pe_f1pair(m, 3)
            s_h1pair(m, 3, g0 + 6)
            pe_attn(b)
            v_evacA(b)
            if t5 < NT:
                ln_bn(t5)
                ln_rstd(t5)
                ln_xn(t5)
            if b2 < NT:
                pe_T(b2, wt_req("vs", ("evacV", a2)))
                v_evacT(b2, buf, 1)
            pe_outproj(a)
            pe_outproj(b)
            tail_blk(0)
            tail_blk(1)
            pe_f1pair(m, 4)
            s_h1pair(m, 4, g0 + 8)
            pe_f1pair(m, 5)
            s_h1pair(m, 5, g0 + 10)
            if b2 < NT:
                pe_v(b2, buf, 1)
                v_evacV(b2)
            for k in range(2, 12):
                tail_blk(k)
                p = k + 4
                pe_f1pair(m, p)
                s_h1pair(m, p, g0 + 2 * p)
                if k == 6 and t6 < NT:
                    ln_bn(t6)
                    ln_rstd(t6)
                    ln_xn(t6)

        gchunk += 32

    if n_macros:
        ml = n_macros - 1
        tb = 2 * n_macros
        for k in range(12, 16):
            pe_f2blk(ml, k, ml * 32 + 2 * k, tb - 1, tb)
        v_out_tt(tb - 1, ml)
        g_accum_store(tb - 1)
        v_out_tt(tb, ml)
        g_accum_store(tb)

    tail = [("sto", 16 * cnt["sto"] // 16)] if cnt["sto"] else []
    tail = [("sto", cnt["sto"])] if cnt["sto"] else []
    for n in ("xin", "pre", "w_win", "w_w1a", "w_w1b", "w_wout",
              "w_w2a", "w_w2b", "w_b1", "w_b2"):
        if cnt[n]:
            tail.append((n, cnt[n]))
    sync_wait(tail)

    if debug_dump:
        dbf = nc.dram_tensor("dbg_bf", [128, 9088], BF16, kind="ExternalOutput").ap()
        dff = nc.dram_tensor("dbg_f32", [128, 26], F32, kind="ExternalOutput").ap()
        pieces_bf = [(ident[:, :], 128), (Pt[:, :, :].rearrange("p a b -> p (a b)"), 1536),
                     (xnb[:, :, :].rearrange("p a b -> p (a b)"), 2048),
                     (vring[:, 0, :], 1024), (attb[:, 0, :], 1024),
                     (h1r[:, 0, :], 256), (WinS[:, 0, 0:1024], 1024),
                     (xnt[:, 0, 0, :], 256), (xnt[:, 1, 0, :], 256),
                     (W1s[:, 0, 0, :], 128), (W2s[:, 0, 0:256], 256),
                     (WoutS[:, 0, 0:256], 256),
                     ]
        off = 0
        k = cnt["sto"]
        for ap, w in pieces_bf:
            def fn(e, ap=ap, o=off, w=w):
                e.dma_start(out=dbf[:, o:o + w], in_=ap).then_inc(S["sto"], 16)
            emit("sync", fn)
            off += w
            k += 16
        def fn2(e):
            e.dma_start(out=dff[:, 0:20],
                        in_=mvs[:, :, :].rearrange("p a b -> p (a b)")).then_inc(S["sto"], 16)
        emit("sync", fn2)
        k += 16
        def fn3(e, k=k):
            e.dma_start(out=dff[:, 20:26], in_=stat[:, 0, 0:6]).then_inc(S["sto"], 16)
            e.wait_ge(S["sto"], k + 16)
        emit("sync", fn3)

    with nc.Block() as init:
        @init.sync
        def _(e):
            for n in sem_names:
                e.sem_clear(S[n])

    with nc.Block(no_gpsimd_drain=True) as blk:
        @blk.tensor
        def _(e):
            for fn in ops["tensor"]:
                fn(e)

        @blk.vector
        def _(e):
            for fn in ops["vector"]:
                fn(e)

        @blk.scalar
        def _(e):
            for fn in ops["scalar"]:
                fn(e)

        @blk.sync
        def _(e):
            for fn in ops["sync"]:
                fn(e)

        @blk.gpsimd
        def _(e):
            for fn in ops["gpsimd"]:
                fn(e)

    return nc


_NC_CACHE = None


def kernel(L0_attn_values, L0_attn_input_weights, L0_attn_output_weight,
           L0_ffn_w1, L0_ffn_b1, L0_ffn_w2, L0_ffn_b2):
    global _NC_CACHE
    x = np.ascontiguousarray(np.asarray(L0_attn_values, dtype=np.float32))
    B, L, Dm = x.shape
    assert (B, L, Dm) == (4, 4096, 1024)

    wint = np.ascontiguousarray(
        np.asarray(L0_attn_input_weights, np.float32).T
        .reshape(8, 128, Dm).transpose(1, 0, 2))
    woutt = np.ascontiguousarray(
        np.asarray(L0_attn_output_weight, np.float32).T
        .reshape(8, 128, Dm).transpose(1, 0, 2))
    w1 = np.ascontiguousarray(
        np.asarray(L0_ffn_w1, np.float32)
        .reshape(8, 128, 32, 128).transpose(1, 2, 0, 3))
    w2 = np.ascontiguousarray(
        np.asarray(L0_ffn_w2, np.float32)
        .reshape(32, 128, Dm).transpose(1, 0, 2))
    b1t = np.ascontiguousarray(np.asarray(L0_ffn_b1, np.float32).reshape(32, 128).T)
    b2bc = np.ascontiguousarray(np.broadcast_to(
        np.asarray(L0_ffn_b2, np.float32).reshape(1, Dm), (128, Dm)))

    if _NC_CACHE is None:
        _NC_CACHE = build_nc()
    nc = _NC_CACHE

    in_maps = []
    for c in range(8):
        bidx, half = divmod(c, 2)
        s = half * 2048
        lo, hi = s - 128, s + 2048 + 128
        slab = np.zeros((NT * 128, Dm), dtype=np.float32)
        clo, chi = max(lo, 0), min(hi, L)
        slab[clo - lo:chi - lo, :] = x[bidx, clo:chi, :]
        in_maps.append({
            "xs": slab, "wint": wint, "woutt": woutt,
            "w1": w1, "w2": w2, "b1t": b1t, "b2bc": b2bc,
        })

    res = run_bass_kernel_spmd(nc, in_maps, core_ids=list(range(8)))

    out = np.empty((4, 4096, 1024), dtype=np.float32)
    for c in range(8):
        bidx, half = divmod(c, 2)
        s = half * 2048
        out[bidx, s:s + 2048, :] = np.asarray(res.results[c]["out"],
                                              dtype=np.float32)
    return out


# version 22 (final)
# speedup vs baseline: 1.2034x; 1.1989x over previous
import math
import numpy as np
import concourse.bass as bass
import concourse.mybir as mybir
from concourse.bass_utils import run_bass_kernel_spmd

F32 = mybir.dt.float32
BF16 = mybir.dt.bfloat16
AF = mybir.ActivationFunctionType
OP = mybir.AluOpType

D = 1024
NT = 18
NMAIN = 16
NMACRO = 8
STD = [1.0, 2.0, 4.0, 8.0]
OFF = [-1.0, -2.0, -4.0, -8.0]
EPS = 1e-5


def build_nc(n_macros=NMACRO, debug_dump=False, scratch=8192):
    nc = bass.Bass("TRN2", target_bir_lowering=False,
                   dynamic_dma_scratch_size=scratch,
                   detect_race_conditions=False)

    xs = nc.dram_tensor("xs", [NT * 128, D], F32, kind="ExternalInput").ap()
    wint = nc.dram_tensor("wint", [128, 8, D], F32, kind="ExternalInput").ap()
    woutt = nc.dram_tensor("woutt", [128, 8, D], F32, kind="ExternalInput").ap()
    w1 = nc.dram_tensor("w1", [128, 32, 8, 128], F32, kind="ExternalInput").ap()
    w2 = nc.dram_tensor("w2", [128, 32, D], F32, kind="ExternalInput").ap()
    b1t = nc.dram_tensor("b1t", [128, 32], F32, kind="ExternalInput").ap()
    b2bc = nc.dram_tensor("b2bc", [128, D], F32, kind="ExternalInput").ap()
    outh = nc.dram_tensor("out", [NMAIN * 128, D], F32, kind="ExternalOutput").ap()

    W1s = nc.alloc_sbuf_tensor("W1s", [128, 32, 8, 128], BF16)
    W2s = nc.alloc_sbuf_tensor("W2s", [128, 32, 1024], BF16)
    WinS = nc.alloc_sbuf_tensor("WinS", [128, 8, 1024], BF16)
    WoutS = nc.alloc_sbuf_tensor("WoutS", [128, 8, 1024], BF16)
    xln = nc.alloc_sbuf_tensor("xln", [128, 3, D], F32)
    outsb = nc.alloc_sbuf_tensor("outsb", [128, 2, D], F32)
    xnb = nc.alloc_sbuf_tensor("xnb", [128, 2, D], BF16)
    xnt = nc.alloc_sbuf_tensor("xnt", [128, 2, 8, 256], BF16)
    vring = nc.alloc_sbuf_tensor("vring", [128, 4, D], BF16)
    attb = nc.alloc_sbuf_tensor("attb", [128, 2, D], BF16)
    h1r = nc.alloc_sbuf_tensor("h1r", [128, 8, 256], BF16)
    Pt = nc.alloc_sbuf_tensor("Pt", [128, 12, 128], BF16)
    ident = nc.alloc_sbuf_tensor("ident", [128, 128], BF16)
    b1s = nc.alloc_sbuf_tensor("b1s", [128, 32], F32)
    b2s = nc.alloc_sbuf_tensor("b2s", [128, D], F32)
    mvs = nc.alloc_sbuf_tensor("mvs", [128, 2, 10], F32)
    stat = nc.alloc_sbuf_tensor("stat", [128, 2, 12], F32)
    lninv = nc.alloc_sbuf_tensor("lninv", [128, 4], F32)

    feat = nc.alloc_psum_tensor("feat", [128, 1024], F32)
    outA = nc.alloc_psum_tensor("outA", [128, 1024], F32)
    outB = nc.alloc_psum_tensor("outB", [128, 1024], F32)
    f1p = [nc.alloc_psum_tensor("f1a", [128, 512], F32),
           nc.alloc_psum_tensor("f1b", [128, 512], F32)]

    sem_names = ["pe", "vs", "sc", "gp", "xin", "pre", "sto",
                 "w_win", "w_w1a", "w_w1b", "w_wout", "w_w2a", "w_w2b",
                 "w_b1", "w_b2"]
    S = {n: nc.alloc_semaphore("sem_" + n) for n in sem_names}

    cnt = {n: 0 for n in sem_names}
    after = {}
    ops = {k: [] for k in ("tensor", "vector", "scalar", "sync", "gpsimd")}

    def emit(eng, fn):
        ops[eng].append(fn)

    def bump(sem, event=None, n=1):
        cnt[sem] += n
        if event is not None:
            assert (sem, event) not in after, event
            after[(sem, event)] = cnt[sem]
        return cnt[sem]

    def wt(sem, event):
        v = after.get((sem, event))
        return None if v is None else (sem, v)

    def wt_req(sem, event):
        v = after[(sem, event)]
        return (sem, v)

    def mk_waits(e, waits):
        for w in waits:
            if w is None:
                continue
            sn, val = w
            e.wait_ge(S[sn], val)

    def t_mms(waits, mms, event=None):
        do_inc = event is not None
        if do_inc:
            bump("pe", event)

        def fn(e, waits=tuple(waits), mms=tuple(mms), do_inc=do_inc):
            mk_waits(e, waits)
            last = None
            for (o, l, r, st, sp) in mms:
                last = e.matmul(o, l, r, start=st, stop=sp,
                                skip_group_check=True)
            if do_inc:
                last.then_inc(S["pe"], 1)
        emit("tensor", fn)

    def v_op(waits, opfn, event=None):
        do_inc = event is not None
        if do_inc:
            bump("vs", event)

        def fn(e, waits=tuple(waits), opfn=opfn, do_inc=do_inc):
            mk_waits(e, waits)
            ins = opfn(e)
            if do_inc:
                if ins is None:
                    ins = e.drain()
                ins.then_inc(S["vs"], 1)
        emit("vector", fn)

    def s_op(waits, opfn, event=None):
        do_inc = event is not None
        if do_inc:
            bump("sc", event)

        def fn(e, waits=tuple(waits), opfn=opfn, do_inc=do_inc):
            mk_waits(e, waits)
            ins = opfn(e)
            if do_inc:
                if ins is None:
                    ins = e.drain()
                ins.then_inc(S["sc"], 1)
        emit("scalar", fn)

    def dma(eng, waits, out_ap, in_ap, sem, event=None, accum=False):
        bump(sem, event, 16)

        def fn(e, waits=tuple(waits), o=out_ap, i=in_ap, sem=sem, accum=accum):
            mk_waits(e, waits)
            if accum:
                e.dma_start(out=o, in_=i, accum_op=OP.add).then_inc(S[sem], 16)
            else:
                e.dma_start(out=o, in_=i).then_inc(S[sem], 16)
        emit(eng, fn)

    def sync_wait(waits, eng="sync"):
        def fn(e, waits=tuple(waits)):
            mk_waits(e, waits)
        emit(eng, fn)

    Gap = outsb[:, 0, 0:384]
    tmp = [outsb[:, 0, 384:512], outsb[:, 0, 640:768]]
    sqr = [outsb[:, 0, 512:640], outsb[:, 0, 768:896]]

    def g_iota(e):
        e.iota(Gap, [[-1, 384]], base=128, channel_multiplier=1,
               allow_small_or_imprecise_dtypes=True)
        e.drain().then_inc(S["gp"], 1)
    bump("gp", "iota")

    w_pieces = [
        (WinS[:, :, 0:512], wint[:, :, 0:512], "w_win"),
        (WinS[:, :, 512:1024], wint[:, :, 512:1024], "w_win"),
        (W1s[:, 0:8, :, :], w1[:, 0:8, :, :], "w_w1a"),
        (WoutS[:, :, :], woutt, "w_wout"),
        (W1s[:, 8:16, :, :], w1[:, 8:16, :, :], "w_w1a"),
        (W2s[:, 0:8, :], w2[:, 0:8, :], "w_w2a"),
        (W1s[:, 16:24, :, :], w1[:, 16:24, :, :], "w_w1b"),
        (W2s[:, 8:16, :], w2[:, 8:16, :], "w_w2a"),
        (W1s[:, 24:32, :, :], w1[:, 24:32, :, :], "w_w1b"),
        (W2s[:, 16:24, :], w2[:, 16:24, :], "w_w2b"),
        (W2s[:, 24:32, :], w2[:, 24:32, :], "w_w2b"),
    ]
    emit("gpsimd", g_iota)
    for o_ap, i_ap, sem in w_pieces:
        dma("gpsimd", [], o_ap, i_ap, sem)

    def ln_bn(t):
        par = t % 2
        waits = [("xin", 16 * (t + 1)), wt("sc", ("rstd", t - 2))]

        def fn(e, par=par, xpar=t % 3):
            mv = mvs[:, par, :]
            e.bn_stats(stat[:, par, 0:6], xln[:, xpar, 0:512])
            e.bn_stats(stat[:, par, 6:12], xln[:, xpar, 512:1024])
            e.drain()
            e.bn_aggr(mv[:, 0:2], stat[:, par, 0:12])
            e.drain()
            return e.tensor_scalar(mv[:, 2:3], mv[:, 1:2], 1.0, EPS,
                                   OP.mult, OP.add)
        v_op(waits, fn, ("bn", t))

    def ln_rstd(t):
        par = t % 2

        def f1(e, par=par):
            e.activation(mvs[:, par, 3:4], mvs[:, par, 2:3],
                         AF.Ln, scale=1.0)
            e.drain()
            return e.activation(mvs[:, par, 4:5], mvs[:, par, 3:4],
                                AF.Exp, scale=-0.5)
        s_op([wt_req("vs", ("bn", t))], f1, ("rstd", t))

    def ln_xn(t):
        par = t % 2
        waits = [wt_req("sc", ("rstd", t)), wt("pe", ("T", t - 2))]

        def fn(e, par=par, xpar=t % 3):
            return e.tensor_scalar(xnb[:, par, :], xln[:, xpar, :],
                                   mvs[:, par, 0:1], mvs[:, par, 4:5],
                                   OP.subtract, OP.mult)
        v_op(waits, fn, ("xn", t))

    def pe_T(t, feat_free_wait, extra=()):
        par = t % 2
        mms = []
        for c in range(8):
            mms.append((feat[:, c * 128:(c + 1) * 128],
                        xnb[:, par, c * 128:(c + 1) * 128],
                        ident[:, :], True, True))
        t_mms([wt_req("vs", ("xn", t)), feat_free_wait,
               wt_req("vs", "ident")] + list(extra), mms, ("T", t))

    def v_evacT(t, buf, pos):
        waits = [wt_req("pe", ("T", t))]
        src = feat[:, :].rearrange("p (c t) -> p c t", t=128)
        dst = xnt[:, buf, :, pos * 128:pos * 128 + 128]

        def fn(e, src=src, dst=dst):
            return e.tensor_copy(dst, src)
        v_op(waits, fn, ("evacT", t))

    def pe_v(t, buf, pos):
        for half in range(2):
            mms = []
            for dc in range(8):
                mms.append((feat[:, half * 512:(half + 1) * 512],
                            xnt[:, buf, dc, pos * 128:pos * 128 + 128],
                            WinS[:, dc, half * 512:(half + 1) * 512],
                            dc == 0, dc == 7))
            t_mms([wt_req("vs", ("evacT", t)), ("w_win", 16 * (half + 1))],
                  mms, ("v", t) if half == 1 else None)

    def v_evacV(t):
        waits = [wt_req("pe", ("v", t)), wt("pe", ("attn", t - 3))]

        def fn(e, t=t):
            return e.tensor_copy(vring[:, t % 4, :], feat[:, :])
        v_op(waits, fn, ("evacV", t))

    def pe_attn(t, extra=()):
        po = outA if t % 2 == 1 else outB
        mms = []
        for h in range(8):
            hc = h % 4
            for di, dlt in enumerate((-1, 0, 1)):
                kt = t + dlt
                idx = hc * 3 + di
                mms.append((po[:, h * 128:(h + 1) * 128],
                            vring[:, kt % 4, h * 128:(h + 1) * 128],
                            Pt[:, idx, :], di == 0, di == 2))
        waits = [wt_req("sc", "pgen"), wt("vs", ("out_tt", t - 2)),
                 wt_req("vs", ("evacV", t + 1))]
        t_mms(waits + list(extra), mms, ("attn", t))

    def v_evacA(t):
        po = outA if t % 2 == 1 else outB
        waits = [wt_req("pe", ("attn", t)), wt("pe", ("op", t - 2))]

        def fn(e, t=t, po=po):
            return e.tensor_copy(attb[:, t % 2, :], po[:, :])
        v_op(waits, fn, ("evacA", t))

    def pe_outproj(t, extra=()):
        po = outA if t % 2 == 1 else outB
        mms = []
        for half in range(2):
            for dc in range(8):
                mms.append((po[:, half * 512:(half + 1) * 512],
                            attb[:, t % 2, dc * 128:(dc + 1) * 128],
                            WoutS[:, dc, half * 512:(half + 1) * 512],
                            dc == 0, False))
        waits = [wt_req("vs", ("evacA", t)), ("w_wout", 16)]
        t_mms(waits + list(extra), mms, ("op", t))

    def pe_f1pair(m, p, extra=()):
        bank = f1p[p % 2]
        buf = (m - 1) % 2 if m >= 1 else 1
        mms = []
        for i in range(2):
            wc = 2 * p + i
            for dc in range(8):
                mms.append((bank[:, i * 256:(i + 1) * 256],
                            W1s[:, wc, dc, :],
                            xnt[:, buf, dc, :],
                            dc == 0, dc == 7))
        waits = list(extra) + [("w_w1a", 16 * (1 if p < 4 else 2)) if p < 8
                               else ("w_w1b", 16 * (1 if p < 12 else 2))]
        if p >= 2:
            waits.append(wt("sc", ("h1", m, p - 2)))
        else:
            waits.append(wt("sc", ("h1", m - 1, p + 14)))
        waits.append(wt("vs", ("evacT", 2 * m + 2)))
        t_mms(waits, mms, ("f1", m, p))

    def s_h1pair(m, p, gchunk0):
        waits = [wt_req("pe", ("f1", m, p)), ("w_b1", 16)]
        for i in range(2):
            g = gchunk0 + i
            if g - 8 >= 0:
                pm, pk = divmod(g - 8, 32)
                waits.append(wt_req("pe", ("f2", pm, pk // 2)))

        def mk(i):
            wc = 2 * p + i
            g = gchunk0 + i

            def fn(e, wc=wc, g=g, p=p):
                return e.activation(
                    h1r[:, g % 8, :],
                    f1p[p % 2][:, (wc % 2) * 256:(wc % 2) * 256 + 256],
                    AF.Relu, bias=b1s[:, wc:wc + 1], scale=1.0)
            return fn
        s_op(waits, mk(0))
        s_op([], mk(1), ("h1", m, p))

    def pe_f2blk(m, k, gchunk0, a, b, extra=()):
        mms = []
        for i in range(2):
            wc = 2 * k + i
            g = gchunk0 + i
            for t in (a, b):
                po = outA if t % 2 == 1 else outB
                pos = 0 if t == a else 1
                for half in range(2):
                    stop = (k == 15) and (i == 1)
                    mms.append((po[:, half * 512:(half + 1) * 512],
                                h1r[:, g % 8, pos * 128:pos * 128 + 128],
                                W2s[:, wc, half * 512:(half + 1) * 512],
                                False, stop))
        t_mms([wt_req("sc", ("h1", m, k))] + list(extra), mms, ("f2", m, k))

    def v_out_tt(t, m):
        waits = [wt_req("pe", ("f2", m, 15)), ("w_b2", 16)]
        if t - 2 >= 1:
            waits.append(("sto", 16 * (t - 2)))
        po = outA if t % 2 == 1 else outB

        def fn(e, t=t, po=po):
            return e.scalar_tensor_tensor(outsb[:, t % 2, :], po[:, :], 0.0,
                                          b2s[:, :], OP.add, OP.add)
        v_op(waits, fn, ("out_tt", t))

    def sync_load(t):
        waits = [wt_req("vs", ("xn", t - 3))] if t - 3 >= 0 else []
        if 1 <= t - 3 <= NMAIN:
            waits.append(("pre", 16 * (t - 3)))
        dma("sync", waits, xln[:, t % 3, :], xs[t * 128:(t + 1) * 128, :], "xin")

    def sync_prestore(t):
        dma("sync", [("xin", 16 * (t + 1))],
            outh[(t - 1) * 128:t * 128, :], xln[:, t % 3, :], "pre")

    def g_accum_store(t):
        dma("gpsimd", [wt_req("vs", ("out_tt", t)), ("pre", 16 * t)],
            outh[(t - 1) * 128:t * 128, :], outsb[:, t % 2, :], "sto",
            accum=True)

    for t in (0, 1, 2):
        dma("sync", [], xln[:, t % 3, :], xs[t * 128:(t + 1) * 128, :], "xin")
    dma("sync", [], b1s[:, :], b1t, "w_b1", "done")
    dma("sync", [], b2s[:, :], b2bc, "w_b2", "done")

    ln_bn(0)

    def mk_ident(e):
        return e.tensor_scalar(ident[:, :], outsb[:, 0, 128:256],
                               0.0, None, OP.is_equal)
    v_op([wt_req("gp", "iota")], mk_ident, "ident")

    def mk_lninv(e):
        for hc in range(4):
            val = math.log(1.0 / (STD[hc] * math.sqrt(2.0 * math.pi)))
            hi = float(np.float32(val))
            lo = val - hi
            e.memset(lninv[:, hc:hc + 1], hi)
            e.drain()
            e.tensor_scalar(lninv[:, hc:hc + 1], lninv[:, hc:hc + 1],
                            lo, None, OP.add)
        return None
    v_op([], mk_lninv, "lninv")

    def s_warm(e):
        e.activation(mvs[:, 0, 5:6], lninv[:, 0:1], AF.Exp, scale=1.0)
    s_op([wt_req("vs", "lninv")], s_warm)

    ln_rstd(0)
    ln_xn(0)

    def psq_round(j):
        hc, di = divmod(j, 3)
        lo, hi = {0: (256, 384), 1: (128, 256), 2: (0, 128)}[di]
        par = j % 2

        def fv(e, hc=hc, lo=lo, hi=hi, par=par):
            e.tensor_scalar(tmp[par], outsb[:, 0, lo:hi],
                            OFF[hc], 1.0 / STD[hc], OP.subtract, OP.mult)
            e.drain()
            return e.tensor_tensor(sqr[par], tmp[par], tmp[par], OP.mult)
        waits = [wt_req("gp", "iota")]
        if j >= 2:
            waits.append(wt_req("sc", ("pexp", j - 2)))
        v_op(waits, fv, ("psq", j))

        def fs(e, j=j, par=par, hc=hc):
            return e.activation(Pt[:, j, :], sqr[par], AF.Exp,
                                bias=lninv[:, hc:hc + 1], scale=-0.5)
        s_op([wt_req("vs", ("psq", j)), wt_req("vs", "lninv")],
             fs, ("pexp", j))

    for _j in range(12):
        psq_round(_j)
    after[("sc", "pgen")] = after[("sc", ("pexp", 11))]

    sync_wait([("xin", 32)])
    sync_prestore(1)
    ln_bn(1)
    ln_rstd(1)
    ln_xn(1)

    pe_T(0, None)
    v_evacT(0, 0, 0)
    pe_v(0, 0, 0)
    v_evacV(0)

    sync_wait([("xin", 48)])
    sync_prestore(2)
    ln_bn(2)
    ln_rstd(2)
    ln_xn(2)

    pe_T(1, wt_req("vs", ("evacV", 0)))
    v_evacT(1, 1, 0)
    pe_v(1, 1, 0)
    v_evacV(1)

    sync_load(3)
    sync_wait([("xin", 64)])
    sync_prestore(3)
    ln_bn(3)
    ln_rstd(3)
    ln_xn(3)

    pe_T(2, wt_req("vs", ("evacV", 1)))
    v_evacT(2, 1, 1)
    pe_v(2, 1, 1)
    v_evacV(2)

    sync_load(4)
    sync_wait([("xin", 80)])
    sync_prestore(4)
    ln_bn(4)
    ln_rstd(4)
    ln_xn(4)

    gchunk = 0
    for m in range(n_macros):
        a, b, a2, b2 = 2 * m + 1, 2 * m + 2, 2 * m + 3, 2 * m + 4
        buf = m % 2
        t5, t6 = 2 * m + 5, 2 * m + 6
        g0 = gchunk

        if t5 < NT:
            sync_load(t5)
            if t5 <= NMAIN:
                sync_wait([("xin", 16 * (t5 + 1))])
                sync_prestore(t5)
        if t6 < NT:
            sync_load(t6)
            if t6 <= NMAIN:
                sync_wait([("xin", 16 * (t6 + 1))])
                sync_prestore(t6)

        def tail_blk(k):
            extra = []
            if m == 0:
                if k == 0:
                    extra.append(("w_w2a", 16))
                elif k == 4:
                    extra.append(("w_w2a", 32))
                elif k == 8:
                    extra.append(("w_w2b", 16))
            pe_f2blk(m, k, g0 + 2 * k, a, b, extra=extra)

        def defer_blk(k):
            extra = [("w_w2b", 32)] if m == 1 else []
            pe_f2blk(m - 1, k, (m - 1) * 32 + 2 * k, a - 2, b - 2,
                     extra=extra)

        if m == 0:
            pe_T(a2, wt_req("vs", ("evacV", b)))
            v_evacT(a2, buf, 0)
            pe_v(a2, buf, 0)
            v_evacV(a2)
            pe_attn(a)
            v_evacA(a)
            pe_attn(b)
            v_evacA(b)
            ln_bn(t5)
            ln_rstd(t5)
            ln_xn(t5)
            pe_T(b2, wt_req("vs", ("evacV", a2)))
            v_evacT(b2, buf, 1)
            pe_v(b2, buf, 1)
            v_evacV(b2)
            for p in range(3):
                pe_f1pair(m, p)
                s_h1pair(m, p, g0 + 2 * p)
            pe_outproj(a)
            pe_outproj(b)
            tail_blk(0)
            tail_blk(1)
            pe_f1pair(m, 3)
            s_h1pair(m, 3, g0 + 6)
            pe_f1pair(m, 4)
            s_h1pair(m, 4, g0 + 8)
            for k in range(2, 12):
                tail_blk(k)
                p = k + 3
                pe_f1pair(m, p)
                s_h1pair(m, p, g0 + 2 * p)
                if k == 6 and t6 < NT:
                    ln_bn(t6)
                    ln_rstd(t6)
                    ln_xn(t6)
            pe_f1pair(m, 15)
            s_h1pair(m, 15, g0 + 30)
        else:
            pe_T(a2, wt_req("vs", ("evacV", b)))
            v_evacT(a2, buf, 0)
            defer_blk(12)
            pe_f1pair(m, 0)
            s_h1pair(m, 0, g0 + 0)
            defer_blk(13)
            pe_f1pair(m, 1)
            s_h1pair(m, 1, g0 + 2)
            pe_v(a2, buf, 0)
            v_evacV(a2)
            defer_blk(14)
            defer_blk(15)
            v_out_tt(a - 2, m - 1)
            g_accum_store(a - 2)
            v_out_tt(b - 2, m - 1)
            g_accum_store(b - 2)
            pe_f1pair(m, 2)
            s_h1pair(m, 2, g0 + 4)
            pe_attn(a)
            v_evacA(a)
            pe_f1pair(m, 3)
            s_h1pair(m, 3, g0 + 6)
            pe_attn(b)
            v_evacA(b)
            if t5 < NT:
                ln_bn(t5)
                ln_rstd(t5)
                ln_xn(t5)
            if b2 < NT:
                pe_T(b2, wt_req("vs", ("evacV", a2)))
                v_evacT(b2, buf, 1)
            pe_outproj(a)
            pe_outproj(b)
            tail_blk(0)
            tail_blk(1)
            pe_f1pair(m, 4)
            s_h1pair(m, 4, g0 + 8)
            pe_f1pair(m, 5)
            s_h1pair(m, 5, g0 + 10)
            if b2 < NT:
                pe_v(b2, buf, 1)
                v_evacV(b2)
            for k in range(2, 12):
                tail_blk(k)
                p = k + 4
                pe_f1pair(m, p)
                s_h1pair(m, p, g0 + 2 * p)
                if k == 6 and t6 < NT:
                    ln_bn(t6)
                    ln_rstd(t6)
                    ln_xn(t6)

        gchunk += 32

    if n_macros:
        ml = n_macros - 1
        tb = 2 * n_macros
        for k in range(12, 16):
            pe_f2blk(ml, k, ml * 32 + 2 * k, tb - 1, tb)
        v_out_tt(tb - 1, ml)
        g_accum_store(tb - 1)
        v_out_tt(tb, ml)
        g_accum_store(tb)

    tail = [("sto", 16 * cnt["sto"] // 16)] if cnt["sto"] else []
    tail = [("sto", cnt["sto"])] if cnt["sto"] else []
    for n in ("xin", "pre", "w_win", "w_w1a", "w_w1b", "w_wout",
              "w_w2a", "w_w2b", "w_b1", "w_b2"):
        if cnt[n]:
            tail.append((n, cnt[n]))
    sync_wait(tail)

    if debug_dump:
        dbf = nc.dram_tensor("dbg_bf", [128, 9088], BF16, kind="ExternalOutput").ap()
        dff = nc.dram_tensor("dbg_f32", [128, 26], F32, kind="ExternalOutput").ap()
        pieces_bf = [(ident[:, :], 128), (Pt[:, :, :].rearrange("p a b -> p (a b)"), 1536),
                     (xnb[:, :, :].rearrange("p a b -> p (a b)"), 2048),
                     (vring[:, 0, :], 1024), (attb[:, 0, :], 1024),
                     (h1r[:, 0, :], 256), (WinS[:, 0, 0:1024], 1024),
                     (xnt[:, 0, 0, :], 256), (xnt[:, 1, 0, :], 256),
                     (W1s[:, 0, 0, :], 128), (W2s[:, 0, 0:256], 256),
                     (WoutS[:, 0, 0:256], 256),
                     ]
        off = 0
        k = cnt["sto"]
        for ap, w in pieces_bf:
            def fn(e, ap=ap, o=off, w=w):
                e.dma_start(out=dbf[:, o:o + w], in_=ap).then_inc(S["sto"], 16)
            emit("sync", fn)
            off += w
            k += 16
        def fn2(e):
            e.dma_start(out=dff[:, 0:20],
                        in_=mvs[:, :, :].rearrange("p a b -> p (a b)")).then_inc(S["sto"], 16)
        emit("sync", fn2)
        k += 16
        def fn3(e, k=k):
            e.dma_start(out=dff[:, 20:26], in_=stat[:, 0, 0:6]).then_inc(S["sto"], 16)
            e.wait_ge(S["sto"], k + 16)
        emit("sync", fn3)

    with nc.Block() as init:
        @init.sync
        def _(e):
            for n in sem_names:
                e.sem_clear(S[n])

    with nc.Block(no_gpsimd_drain=True) as blk:
        @blk.tensor
        def _(e):
            for fn in ops["tensor"]:
                fn(e)

        @blk.vector
        def _(e):
            for fn in ops["vector"]:
                fn(e)

        @blk.scalar
        def _(e):
            for fn in ops["scalar"]:
                fn(e)

        @blk.sync
        def _(e):
            for fn in ops["sync"]:
                fn(e)

        @blk.gpsimd
        def _(e):
            for fn in ops["gpsimd"]:
                fn(e)

    return nc


_NC_CACHE = None


def kernel(L0_attn_values, L0_attn_input_weights, L0_attn_output_weight,
           L0_ffn_w1, L0_ffn_b1, L0_ffn_w2, L0_ffn_b2):
    global _NC_CACHE
    x = np.ascontiguousarray(np.asarray(L0_attn_values, dtype=np.float32))
    B, L, Dm = x.shape
    assert (B, L, Dm) == (4, 4096, 1024)

    wint = np.ascontiguousarray(
        np.asarray(L0_attn_input_weights, np.float32).T
        .reshape(8, 128, Dm).transpose(1, 0, 2))
    woutt = np.ascontiguousarray(
        np.asarray(L0_attn_output_weight, np.float32).T
        .reshape(8, 128, Dm).transpose(1, 0, 2))
    w1 = np.ascontiguousarray(
        np.asarray(L0_ffn_w1, np.float32)
        .reshape(8, 128, 32, 128).transpose(1, 2, 0, 3))
    w2 = np.ascontiguousarray(
        np.asarray(L0_ffn_w2, np.float32)
        .reshape(32, 128, Dm).transpose(1, 0, 2))
    b1t = np.ascontiguousarray(np.asarray(L0_ffn_b1, np.float32).reshape(32, 128).T)
    b2bc = np.ascontiguousarray(np.broadcast_to(
        np.asarray(L0_ffn_b2, np.float32).reshape(1, Dm), (128, Dm)))

    if _NC_CACHE is None:
        _NC_CACHE = build_nc()
    nc = _NC_CACHE

    in_maps = []
    for c in range(8):
        bidx, half = divmod(c, 2)
        s = half * 2048
        lo, hi = s - 128, s + 2048 + 128
        slab = np.zeros((NT * 128, Dm), dtype=np.float32)
        clo, chi = max(lo, 0), min(hi, L)
        slab[clo - lo:chi - lo, :] = x[bidx, clo:chi, :]
        in_maps.append({
            "xs": slab, "wint": wint, "woutt": woutt,
            "w1": w1, "w2": w2, "b1t": b1t, "b2bc": b2bc,
        })

    res = run_bass_kernel_spmd(nc, in_maps, core_ids=list(range(8)))

    out = np.empty((4, 4096, 1024), dtype=np.float32)
    for c in range(8):
        bidx, half = divmod(c, 2)
        s = half * 2048
        out[bidx, s:s + 2048, :] = np.asarray(res.results[c]["out"],
                                              dtype=np.float32)
    return out
